# version 01
# baseline (speedup 1.0000x reference)
import contextlib
import math
import numpy as np
import ml_dtypes
import concourse.bass as bass
import concourse.mybir as mybir
from concourse.bass_utils import run_bass_kernel_spmd

F32 = mybir.dt.float32
BF16 = mybir.dt.bfloat16
I32 = mybir.dt.int32
AF = mybir.ActivationFunctionType
ALU = mybir.AluOpType
AX = mybir.AxisListType
BF = ml_dtypes.bfloat16

D = 2048
L = 2048
NH = 16
DH = 128
DHR = 64
DCQ = 1536
DC = 512
I0 = 2048
I1 = 5504
SCALE0 = 1.0 / math.sqrt(128.0)
SCALE1 = 1.0 / math.sqrt(128.0)
EPS = 1e-5

FUSED = False

SHAPES = {
    "x":        ([L, D], F32),
    "wdq":      ([D, DCQ], F32),
    "wuq":      ([DCQ, NH * DH], F32),
    "wqr":      ([DCQ, NH * DHR], F32),
    "wdkv":     ([D, DC], F32),
    "wuk":      ([DC, NH * DH], F32),
    "wkr":      ([D, NH * DHR], F32),
    "wuv":      ([DC, NH * DH], F32),
    "wo0":      ([NH * DH, D], F32),
    "gup":      ([4, D, 2 * I0], F32),
    "dwn":      ([4, I0, D], F32),
    "wqkvT":    ([D, 3 * NH * DH], F32),
    "wo1T":     ([NH * DH, D], F32),
    "lut":      ([1, 16], F32),
    "gidxT":    ([D, I1], I32),
    "uidxT":    ([D, I1], I32),
    "didxT":    ([I1, D], I32),
    "identf":   ([128, 128], F32),
    "identb":   ([128, 128], BF16),
    "onescol":  ([128, 1], BF16),
    "onesrowf": ([1, 128], F32),
    "rope0c":   ([128, L], BF16),
    "rope0s":   ([128, L], BF16),
    "rope1c":   ([128, L], BF16),
    "rope1s":   ([128, L], BF16),
    "maskdiag": ([128, 128], BF16),
    "maskleft": ([128, 128], BF16),
    "xT":       ([D, L], F32),
    "xnT":      ([D, L], BF16),
    "qT":       ([NH * DH + NH * DHR, L], BF16),
    "kT":       ([NH * DH + NH * DHR, L], BF16),
    "vT":       ([NH * DH, L], BF16),
    "vtok":     ([L, NH * DH], BF16),
    "qnrm":     ([1, 2 * NH], F32),
    "knrm":     ([1, 2 * NH], F32),
    "aoT0":     ([D, L], BF16),
    "g0":       ([D, L], F32),
    "x1T":      ([D, L], F32),
    "st1":      ([2, L], F32),
    "xn1T":     ([D, L], BF16),
    "qkvT":     ([3 * NH * DH, L], BF16),
    "v1tok":    ([L, NH * DH], BF16),
    "qnrm1":    ([1, NH], F32),
    "knrm1":    ([1, NH], F32),
    "aoT1":     ([D, L], BF16),
    "hT":       ([I1, L], BF16),
    "g1":       ([D, L], F32),
    "y":        ([L, D], F32),
}


def _rope_tables(dim, base=10000.0, seq=L):
    half = dim // 2
    inv = 1.0 / base ** (np.arange(0, dim, 2, dtype=np.float64) / dim)
    ang = np.arange(seq, dtype=np.float64)[None, :] * inv[:, None]
    c = np.cos(ang)
    s = np.sin(ang)
    ct = np.concatenate([c, c], axis=0)
    st = np.concatenate([-s, s], axis=0)
    return ct.astype(BF), st.astype(BF)


def _constants():
    c = {}
    c["identf"] = np.eye(128, dtype=np.float32)
    c["identb"] = np.eye(128).astype(BF)
    c["onescol"] = np.ones((128, 1), dtype=BF)
    c["onesrowf"] = np.ones((1, 128), dtype=np.float32)
    c0, s0 = _rope_tables(64)
    c["rope0c"], c["rope0s"] = np.tile(c0, (2, 1)), np.tile(s0, (2, 1))
    c["rope1c"], c["rope1s"] = _rope_tables(128)
    ii = np.arange(128)
    c["maskdiag"] = (ii[:, None] <= ii[None, :]).astype(BF)
    c["maskleft"] = (ii[:, None] > ii[None, :]).astype(BF)
    return c


class Sem:
    __slots__ = ("h", "n")

    def __init__(self, h):
        self.h = h
        self.n = 0


class Ctx:
    def __init__(self, nc, es_prog, block):
        self.nc = nc
        self.es_prog = es_prog
        self.es = es_prog
        self.block = block
        self._n = 0
        self.ords = {}
        self.marker = {}
        self.final = []
        self.bar = []

    def sem(self, name=None):
        self._n += 1
        return Sem(self.es_prog.enter_context(
            self.nc.semaphore(name or f"s{self._n}")))

    def sb(self, name, shape, dt):
        self._n += 1
        return self.es.enter_context(self.nc.sbuf_tensor(f"{name}{self._n}", shape, dt))

    def ps(self, name, shape, dt=F32):
        self._n += 1
        return self.es.enter_context(self.nc.psum_tensor(f"{name}{self._n}", shape, dt))


def chain(c, key, eng, ins, sem=None):
    if sem is None:
        if key not in c.ords:
            c.ords[key] = c.sem(f"ord_{key}_{c._n}")
        sem = c.ords[key]
    ins.then_inc(sem.h, 1)
    sem.n += 1
    c.marker[key] = (sem, sem.n)
    return sem


def wchain(c, key, eng):
    m = c.marker.get(key)
    if m is not None:
        eng.wait_ge(m[0].h, m[1])


@contextlib.contextmanager
def scope(c):
    old = c.es
    with contextlib.ExitStack() as es:
        c.es = es
        yield
    c.es = old


def barrier(c, engines=("sync", "gpsimd", "tensor", "scalar", "vector")):
    conds = list(c.bar)
    for m in c.marker.values():
        conds.append((m[0], m[1]))
    blk = c.block
    for en in engines:
        def _w(eng, conds=conds):
            for sem, n in conds:
                eng.wait_ge(sem.h, n)
        getattr(blk, en)(_w)
    c.bar = []


def dma(q, out, in_, sem):
    q.dma_start(out=out, in_=in_).then_inc(sem.h, 16)
    sem.n += 16
    return sem.n


def col_block_ap(t, rows, n0, nw):
    return t[0:rows, n0:n0 + nw].rearrange("(kc p) n -> p kc n", p=128)


def emit_gemm(c, *, name, wt, K, M, rhs, Nt=L, out=None, out_sb=None,
              mblock=512, nbanks=4, evac="copy", wt_cast=True, nw=512):
    nc, blk = c.nc, c.block
    Kc = K // 128
    n_mb = (M + mblock - 1) // mblock
    mb_sizes = [min(mblock, M - i * mblock) for i in range(n_mb)]
    nws = [min(nw, Nt - i * nw) for i in range((Nt + nw - 1) // nw)]

    wbuf = c.sb(f"{name}_w", [128, 2 * Kc * mblock], BF16)
    wload = [c.sem(), c.sem()]
    pready = c.sem()
    pfree = [c.sem() for _ in range(nbanks)]
    banks = [c.ps(f"{name}_ps", [128, nw], F32) for _ in range(nbanks)]
    if out is not None:
        strip = c.sb(f"{name}_st", [128, 2 * Nt], BF16)
        sfree = [c.sem(), c.sem()]

    wq = blk.gpsimd if wt_cast else blk.sync

    cum_groups = [0]
    for mb in range(n_mb):
        cum_groups.append(cum_groups[-1] + (mb_sizes[mb] // 128) * len(nws))

    def _loads(q):
        for mb in range(n_mb):
            if mb >= 2:
                q.wait_ge(pready.h, cum_groups[mb - 1])
            par = mb % 2
            m0 = mb * mblock
            mw = mb_sizes[mb]
            src = wt[0:K, m0:m0 + mw].rearrange("(kc p) m -> p kc m", p=128)
            dst = wbuf[:, par * Kc * mblock:par * Kc * mblock + Kc * mw] \
                .rearrange("p (kc m) -> p kc m", m=mw)
            dma(q, dst, src, wload[par])
    wq(_loads)

    @blk.tensor
    def _(t):
        gi = 0
        for mb in range(n_mb):
            par = mb % 2
            t.wait_ge(wload[par].h, 16 * (mb // 2 + 1))
            for ms in range(mb_sizes[mb] // 128):
                for ni, nwi in enumerate(nws):
                    b = gi % nbanks
                    if gi >= nbanks:
                        t.wait_ge(pfree[b].h, gi // nbanks)
                    for kc in range(Kc):
                        lhsT = wbuf[:, par * Kc * mblock + kc * mb_sizes[mb]
                                    + ms * 128:par * Kc * mblock + kc * mb_sizes[mb]
                                    + ms * 128 + 128]
                        ins = t.matmul(banks[b][:, 0:nwi], lhsT,
                                       rhs(kc, ni * nw, nwi),
                                       start=(kc == 0), stop=(kc == Kc - 1))
                    ins.then_inc(pready.h, 1)
                    pready.n += 1
                    gi += 1

    @blk.scalar
    def _(s):
        gi = 0
        for mb in range(n_mb):
            for ms in range(mb_sizes[mb] // 128):
                mi = mb * (mblock // 128) + ms
                for ni, nwi in enumerate(nws):
                    b = gi % nbanks
                    s.wait_ge(pready.h, gi + 1)
                    if out is not None:
                        if ni == 0 and mi >= 2:
                            s.wait_ge(sfree[mi % 2].h, 16 * (mi // 2))
                        dst = strip[:, (mi % 2) * Nt + ni * nw:
                                    (mi % 2) * Nt + ni * nw + nwi]
                    else:
                        dst = out_sb[:, mi * Nt + ni * nw:mi * Nt + ni * nw + nwi]
                    wchain(c, "s", s)
                    if evac == "copy":
                        ins = s.copy(dst, banks[b][:, 0:nwi])
                    elif evac == "silu":
                        ins = s.activation(dst, banks[b][:, 0:nwi], AF.Silu)
                    else:
                        ins = evac(s, mi, ni, banks[b][:, 0:nwi], dst)
                    chain(c, "s", s, ins, sem=pfree[b])
                    gi += 1
                if out is not None:
                    m = c.marker["s"]
                    s.wait_ge(m[0].h, m[1])
                    dma(s, out[mi * 128:(mi + 1) * 128, 0:Nt],
                        strip[:, (mi % 2) * Nt:(mi % 2) * Nt + Nt], sfree[mi % 2])

    c.bar.append((pready, pready.n))
    for b in range(nbanks):
        c.bar.append((pfree[b], pfree[b].n))
    if out is not None:
        c.final.append((sfree[0], sfree[0].n))
        c.final.append((sfree[1], sfree[1].n))
        c.bar.append((sfree[0], sfree[0].n))
        c.bar.append((sfree[1], sfree[1].n))


def emit_preload(c, q_sel, dram, R, Cc, sb, sem, cast=True):
    src = dram[0:R, 0:Cc].rearrange("(kc p) m -> p kc m", p=128)
    dst = sb[:, 0:(R // 128) * Cc].rearrange("p (kc m) -> p kc m", m=Cc)
    return dma(q_sel, dst, src, sem)


def finish(c):
    @c.block.sync
    def _(q):
        for sem, n in c.final:
            q.wait_ge(sem.h, n)
    c.final = []


def p_prep(c, t):
    nc, blk = c.nc, c.block
    ld = [c.sem("ld0"), c.sem("ld1")]
    idld = c.sem("idld")
    sqd = c.sem("sqd")
    red = c.sem("red")
    sqr = c.sem("sqr")
    trd = c.sem("trd")
    evd = c.sem("evd")
    st = [c.sem("st0"), c.sem("st1s")]

    xs = c.sb("xs", [128, 2 * D], F32)
    sq = c.sb("sq", [128, 2 * D], F32)
    xn = c.sb("xn", [128, 2 * D], BF16)
    idn = c.sb("idn", [128, 128], F32)
    idnb = c.sb("idnb", [128, 128], BF16)
    cols = c.sb("cols", [128, 8], F32)
    ostrip = c.sb("ostrip", [128, 2 * 16 * 128], F32)
    nstrip = c.sb("nstrip", [128, 2 * 16 * 128], BF16)
    tp = [c.ps("tp", [128, 128], F32) for _ in range(2)]
    tpb = [c.ps("tpb", [128, 128], BF16) for _ in range(2)]

    @blk.sync
    def _(q):
        dma(q, idn[:], t["identf"][:], idld)
        dma(q, idnb[:], t["identb"][:], idld)
        for tc in range(16):
            if tc >= 2:
                q.wait_ge(red.h, 2 * (tc - 1))
                q.wait_ge(trd.h, 32 * (tc - 1))
            dma(q, xs[:, (tc % 2) * D:(tc % 2) * D + D],
                t["x"][tc * 128:(tc + 1) * 128, :], ld[tc % 2])

    @blk.scalar
    def _(s):
        def A(fn):
            wchain(c, "s", s)
            chain(c, "s", s, fn())
        for tc in range(16):
            p = (tc % 2) * D
            s.wait_ge(ld[tc % 2].h, 16 * (tc // 2 + 1))
            if tc >= 2:
                s.wait_ge(red.h, 2 * tc - 3)
            wchain(c, "s", s)
            chain(c, "s", s, s.activation(sq[:, p:p + D], xs[:, p:p + D],
                                          AF.Square), sem=sqd)
            s.wait_ge(red.h, 2 * tc + 1)
            wchain(c, "s", s)
            chain(c, "s", s, s.activation(cols[:, 6:7], cols[:, 5:6], AF.Sqrt),
                  sem=sqr)
            if tc >= 2:
                s.wait_ge(st[tc % 2].h, 32 * (tc // 2))
            for dc in range(16):
                j = tc * 16 + dc
                s.wait_ge(trd.h, 2 * j + 1)
                wchain(c, "s", s)
                chain(c, "s", s, s.copy(ostrip[:, p + dc * 128:p + dc * 128 + 128],
                                        tp[dc % 2][:]), sem=evd)
                s.wait_ge(trd.h, 2 * j + 2)
                wchain(c, "s", s)
                chain(c, "s", s, s.copy(nstrip[:, p + dc * 128:p + dc * 128 + 128],
                                        tpb[dc % 2][:]), sem=evd)
            s.wait_ge(evd.h, 32 * (tc + 1))
            dma(s, t["xT"][:, tc * 128:(tc + 1) * 128]
                .rearrange("(dc p) n -> p dc n", p=128),
                ostrip[:, p:p + 2048].rearrange("p (dc n) -> p dc n", n=128),
                st[tc % 2])
            dma(s, t["xnT"][:, tc * 128:(tc + 1) * 128]
                .rearrange("(dc p) n -> p dc n", p=128),
                nstrip[:, p:p + 2048].rearrange("p (dc n) -> p dc n", n=128),
                st[tc % 2])

    @blk.vector
    def _(v):
        def V(fn):
            wchain(c, "v", v)
            chain(c, "v", v, fn())
        for tc in range(16):
            p = (tc % 2) * D
            v.wait_ge(ld[tc % 2].h, 16 * (tc // 2 + 1))
            V(lambda: v.tensor_reduce(cols[:, 0:1], xs[:, p:p + D], axis=AX.X,
                                      op=ALU.add))
            v.wait_ge(sqd.h, tc + 1)
            V(lambda: v.tensor_reduce(cols[:, 1:2], sq[:, p:p + D], axis=AX.X,
                                      op=ALU.add))
            V(lambda: v.tensor_scalar(cols[:, 2:3], cols[:, 0:1], 1.0 / D, None,
                                      op0=ALU.mult))
            V(lambda: v.tensor_scalar(cols[:, 3:4], cols[:, 1:2], 1.0 / D, None,
                                      op0=ALU.mult))
            V(lambda: v.tensor_tensor(cols[:, 4:5], cols[:, 2:3], cols[:, 2:3],
                                      op=ALU.mult))
            V(lambda: v.tensor_tensor(cols[:, 5:6], cols[:, 3:4], cols[:, 4:5],
                                      op=ALU.subtract))
            wchain(c, "v", v)
            chain(c, "v", v, v.tensor_scalar(cols[:, 5:6], cols[:, 5:6], EPS,
                                             None, op0=ALU.add), sem=red)
            v.wait_ge(sqr.h, tc + 1)
            V(lambda: v.reciprocal(cols[:, 7:8], cols[:, 6:7]))
            wchain(c, "v", v)
            chain(c, "v", v, v.tensor_scalar(xn[:, p:p + D], xs[:, p:p + D],
                                             cols[:, 2:3], cols[:, 7:8],
                                             op0=ALU.subtract, op1=ALU.mult),
                  sem=red)

    @blk.tensor
    def _(tt):
        tt.wait_ge(idld.h, 32)
        for tc in range(16):
            p = (tc % 2) * D
            tt.wait_ge(red.h, 2 * tc + 2)
            for dc in range(16):
                j = tc * 16 + dc
                if j >= 2:
                    tt.wait_ge(evd.h, 2 * j - 2)
                tt.transpose(tp[dc % 2][:], xs[:, p + dc * 128:p + dc * 128 + 128],
                             idn[:]).then_inc(trd.h, 1)
                trd.n += 1
                tt.transpose(tpb[dc % 2][:], xn[:, p + dc * 128:p + dc * 128 + 128],
                             idnb[:]).then_inc(trd.h, 1)
                trd.n += 1

    c.final.append((st[0], st[0].n))
    c.final.append((st[1], st[1].n))
    finish(c)


def emit_rope(c, t, rows_dram, ctab_sb, stab_sb, nchunks):
    nc, blk = c.nc, c.block
    half = 32 if ctab_sb is not None and False else None
    raise NotImplementedError


def _emit_rope_pass(c, t, rows_dram, nchunks, ctab, stab, swap):
    nc, blk = c.nc, c.block
    ld = [c.sem(), c.sem()]
    std = [c.sem(), c.sem()]
    rdone = c.sem()
    rbuf = c.sb("ropebuf", [128, 2 * L], BF16)
    swp = c.sb("ropeswp", [128, L], BF16)
    t1 = c.sb("ropet1", [128, L], BF16)

    @blk.sync
    def _(q):
        for ch in range(nchunks):
            if ch >= 2:
                q.wait_ge(std[ch % 2].h, 16 * (ch // 2))
            dma(q, rbuf[:, (ch % 2) * L:(ch % 2) * L + L],
                rows_dram[ch * 128:(ch + 1) * 128, :], ld[ch % 2])

    @blk.vector
    def _(v):
        for ch in range(nchunks):
            p = (ch % 2) * L
            v.wait_ge(ld[ch % 2].h, 16 * (ch // 2 + 1))
            if ch >= 2:
                v.wait_ge(std[ch % 2].h, 16 * (ch // 2))
            for g0 in range(0, 128, 2 * swap):
                wchain(c, "v", v)
                chain(c, "v", v, v.tensor_copy(swp[g0:g0 + swap, :],
                                               rbuf[g0 + swap:g0 + 2 * swap, p:p + L]))
                wchain(c, "v", v)
                chain(c, "v", v, v.tensor_copy(swp[g0 + swap:g0 + 2 * swap, :],
                                               rbuf[g0:g0 + swap, p:p + L]))
            wchain(c, "v", v)
            chain(c, "v", v, v.tensor_tensor(t1[:], rbuf[:, p:p + L], ctab[:],
                                             op=ALU.mult))
            wchain(c, "v", v)
            chain(c, "v", v, v.tensor_tensor(swp[:], swp[:], stab[:], op=ALU.mult))
            wchain(c, "v", v)
            chain(c, "v", v, v.tensor_tensor(rbuf[:, p:p + L], t1[:], swp[:],
                                             op=ALU.add), sem=rdone)

    @blk.scalar
    def _(s):
        for ch in range(nchunks):
            p = (ch % 2) * L
            s.wait_ge(rdone.h, ch + 1)
            dma(s, rows_dram[ch * 128:(ch + 1) * 128, :],
                rbuf[:, p:p + L], std[ch % 2])
    c.final.append((std[0], std[0].n))
    c.final.append((std[1], std[1].n))
    c.bar.append((std[0], std[0].n))
    c.bar.append((std[1], std[1].n))
    c.bar.append((rdone, rdone.n))


def _emit_transpose_pass(c, t, src_dram, dst_dram, nchunks, ntc):
    nc, blk = c.nc, c.block
    ld = [c.sem(), c.sem()]
    std = [c.sem(), c.sem()]
    trd = c.sem()
    evd = c.sem()
    tbuf = c.sb("tpin", [128, 2 * L], BF16)
    osb = c.sb("tpout", [128, 2 * L], BF16)
    idnb = c.sb("tpid", [128, 128], BF16)
    idld = c.sem()
    tps = [c.ps("tpp", [128, 128], BF16) for _ in range(2)]

    @blk.sync
    def _(q):
        dma(q, idnb[:], t["identb"][:], idld)
        for ch in range(nchunks):
            if ch >= 2:
                q.wait_ge(trd.h, ntc * (ch - 1))
            dma(q, tbuf[:, (ch % 2) * L:(ch % 2) * L + ntc * 128],
                src_dram[ch * 128:(ch + 1) * 128, :], ld[ch % 2])

    @blk.tensor
    def _(tt):
        tt.wait_ge(idld.h, 16)
        for ch in range(nchunks):
            p = (ch % 2) * L
            tt.wait_ge(ld[ch % 2].h, 16 * (ch // 2 + 1))
            for tc in range(ntc):
                j = ch * ntc + tc
                if j >= 2:
                    tt.wait_ge(evd.h, j - 1)
                tt.transpose(tps[j % 2][:], tbuf[:, p + tc * 128:p + tc * 128 + 128],
                             idnb[:]).then_inc(trd.h, 1)
                trd.n += 1

    @blk.scalar
    def _(s):
        for ch in range(nchunks):
            p = (ch % 2) * L
            if ch >= 2:
                s.wait_ge(std[ch % 2].h, 16 * (ch // 2))
            for tc in range(ntc):
                j = ch * ntc + tc
                s.wait_ge(trd.h, j + 1)
                wchain(c, "s", s)
                chain(c, "s", s, s.copy(osb[:, p + tc * 128:p + tc * 128 + 128],
                                        tps[j % 2][:]), sem=evd)
            m = c.marker["s"]
            s.wait_ge(m[0].h, m[1])
            dma(s, dst_dram[:, ch * 128:(ch + 1) * 128]
                .rearrange("(tc p) n -> p tc n", p=128),
                osb[:, p:p + ntc * 128].rearrange("p (tc n) -> p tc n", n=128),
                std[ch % 2])
    c.final.append((std[0], std[0].n))
    c.final.append((std[1], std[1].n))
    c.bar.append((std[0], std[0].n))
    c.bar.append((std[1], std[1].n))
    c.bar.append((trd, trd.n))
    c.bar.append((evd, evd.n))


def p_l0proj(c, t):
    nc, blk = c.nc, c.block
    xnld = c.sem()
    xnT_sb = c.sb("xnTsb", [128, 16 * L], BF16)
    qc_sb = c.sb("qcsb", [128, 12 * L], BF16)
    cT_sb = c.sb("cTsb", [128, 4 * L], BF16)
    rc = c.sb("ropec", [128, L], BF16)
    rs = c.sb("ropes", [128, L], BF16)

    @blk.sync
    def _(q):
        emit_preload(c, q, t["xnT"], D, L, xnT_sb, xnld, cast=False)
        dma(q, rc[:], t["rope0c"][:], xnld)
        dma(q, rs[:], t["rope0s"][:], xnld)

    @blk.tensor
    def _(tt):
        tt.wait_ge(xnld.h, 48)
    @blk.gpsimd
    def _(g):
        g.wait_ge(xnld.h, 48)

    xn = lambda kc, n0, nw: xnT_sb[:, kc * L + n0:kc * L + n0 + nw]
    qc = lambda kc, n0, nw: qc_sb[:, kc * L + n0:kc * L + n0 + nw]
    cT = lambda kc, n0, nw: cT_sb[:, kc * L + n0:kc * L + n0 + nw]

    with scope(c):
        emit_gemm(c, name="qc", wt=t["wdq"], K=D, M=DCQ, rhs=xn, out_sb=qc_sb)
    barrier(c)
    with scope(c):
        emit_gemm(c, name="cc", wt=t["wdkv"], K=D, M=DC, rhs=xn, out_sb=cT_sb)
    barrier(c)
    with scope(c):
        emit_gemm(c, name="qb", wt=t["wuq"], K=DCQ, M=NH * DH, rhs=qc,
                  out=t["qT"][0:NH * DH, :])
    barrier(c)
    with scope(c):
        emit_gemm(c, name="qr", wt=t["wqr"], K=DCQ, M=NH * DHR, rhs=qc,
                  out=t["qT"][NH * DH:NH * DH + NH * DHR, :])
    barrier(c)
    with scope(c):
        emit_gemm(c, name="kb", wt=t["wuk"], K=DC, M=NH * DH, rhs=cT,
                  out=t["kT"][0:NH * DH, :])
    barrier(c)
    with scope(c):
        emit_gemm(c, name="kr", wt=t["wkr"], K=D, M=NH * DHR, rhs=xn,
                  out=t["kT"][NH * DH:NH * DH + NH * DHR, :])
    barrier(c)
    with scope(c):
        emit_gemm(c, name="vv", wt=t["wuv"], K=DC, M=NH * DH, rhs=cT,
                  out=t["vT"][0:NH * DH, :])
    barrier(c)
    with scope(c):
        _emit_rope_pass(c, t, t["qT"][NH * DH:NH * DH + NH * DHR, :], 8, rc, rs, 32)
    barrier(c)
    with scope(c):
        _emit_rope_pass(c, t, t["kT"][NH * DH:NH * DH + NH * DHR, :], 8, rc, rs, 32)
    barrier(c)
    with scope(c):
        _emit_transpose_pass(c, t, t["vT"], t["vtok"], 16, 16)
    finish(c)


def p_l0ffn(c, t):
    nc, blk = c.nc, c.block
    xnld = [c.sem(), c.sem()]
    xne_sb = c.sb("ffnxne", [128, 4 * 16 * 512], BF16)
    inter_sb = c.sb("ffnint", [128, 4 * 16 * 512], BF16)
    xnstr = c.sb("ffnxns", [128, 2 * L], BF16)

    gat = c.sem()
    @blk.sync
    def _(q):
        for kc in range(16):
            if kc >= 2:
                q.wait_ge(gat.h, 4 * (kc - 1))
            dma(q, xnstr[:, (kc % 2) * L:(kc % 2) * L + L],
                t["xnT"][kc * 128:(kc + 1) * 128, :], xnld[kc % 2])

    @blk.vector
    def _(v):
        for kc in range(16):
            p = (kc % 2) * L
            v.wait_ge(xnld[kc % 2].h, 16 * (kc // 2 + 1))
            for e in range(4):
                wchain(c, "v", v)
                chain(c, "v", v, v.tensor_copy(
                    xne_sb[:, (e * 16 + kc) * 512:(e * 16 + kc) * 512 + 512],
                    xnstr[:, p + e: p + 2048: 4]), sem=gat)

    @blk.tensor
    def _(tt):
        tt.wait_ge(gat.h, 64)
    @blk.gpsimd
    def _(g):
        g.wait_ge(gat.h, 64)

    gu_sb = c.sb("ffngu", [128, 32 * 512], BF16)
    for e in range(4):
        xne = lambda kc, n0, nw, e=e: xne_sb[:, (e * 16 + kc) * 512 + n0:
                                             (e * 16 + kc) * 512 + n0 + nw]
        def guevac(s, mi, ni, ps, dst):
            if mi < 16:
                return s.activation(dst, ps, AF.Silu)
            return s.copy(dst, ps)
        with scope(c):
            emit_gemm(c, name=f"gu{e}", wt=t["gup"][e], K=D, M=2 * I0,
                      rhs=xne, Nt=512, out_sb=gu_sb, evac=guevac)
        barrier(c)
        mul = c.sem()
        @blk.vector
        def _(v):
            for mi in range(16):
                wchain(c, "v", v)
                chain(c, "v", v, v.tensor_tensor(
                    inter_sb[:, (e * 16 + mi) * 512:(e * 16 + mi) * 512 + 512],
                    gu_sb[:, mi * 512:mi * 512 + 512],
                    gu_sb[:, (16 + mi) * 512:(16 + mi) * 512 + 512],
                    op=ALU.mult), sem=mul)
        c.bar.append((mul, mul.n))
        barrier(c)

    wld = [c.sem(), c.sem()]
    pr = c.sem()
    pf = [c.sem() for _ in range(4)]
    gsc = c.sem()
    gst = [c.sem(), c.sem()]
    wbuf = c.sb("dwnw", [128, 2 * 16 * 128], BF16)
    gtile = c.sb("gt", [128, 2 * 512], F32)
    gstrip = c.sb("gstr", [128, 2 * L], F32)
    banks = [c.ps("dps", [128, 512], F32) for _ in range(4)]

    @blk.gpsimd
    def _(g):
        for it in range(64):
            mi, e = divmod(it, 4)
            if it >= 2:
                g.wait_ge(pr.h, it - 1)
            par = it % 2
            src = t["dwn"][e][0:I0, mi * 128:(mi + 1) * 128] \
                .rearrange("(kc p) m -> p kc m", p=128)
            dst = wbuf[:, par * 16 * 128:par * 16 * 128 + 16 * 128] \
                .rearrange("p (kc m) -> p kc m", m=128)
            dma(g, dst, src, wld[par])

    @blk.tensor
    def _(tt):
        for it in range(64):
            mi, e = divmod(it, 4)
            par = it % 2
            tt.wait_ge(wld[par].h, 16 * (it // 2 + 1))
            b = it % 4
            if it >= 4:
                tt.wait_ge(pf[b].h, it // 4)
            for kc in range(16):
                ins = tt.matmul(banks[b][:],
                                wbuf[:, par * 2048 + kc * 128:par * 2048 + kc * 128 + 128],
                                inter_sb[:, (e * 16 + kc) * 512:(e * 16 + kc) * 512 + 512],
                                start=(kc == 0), stop=(kc == 15))
            ins.then_inc(pr.h, 1)
            pr.n += 1

    @blk.scalar
    def _(s):
        for it in range(64):
            mi, e = divmod(it, 4)
            b = it % 4
            s.wait_ge(pr.h, it + 1)
            if it >= 2:
                s.wait_ge(gsc.h, it - 1)
            wchain(c, "s", s)
            chain(c, "s", s, s.activation(gtile[:, (it % 2) * 512:(it % 2) * 512 + 512],
                                          banks[b][:], AF.Silu), sem=pf[b])
            if e == 3:
                s.wait_ge(gsc.h, it + 1)
                dma(s, t["g0"][mi * 128:(mi + 1) * 128, :],
                    gstrip[:, (mi % 2) * L:(mi % 2) * L + L], gst[mi % 2])

    @blk.vector
    def _(v):
        for it in range(64):
            mi, e = divmod(it, 4)
            b = it % 4
            m = pf[b]
            v.wait_ge(m.h, (it // 4) + 1)
            if e == 0 and mi >= 2:
                v.wait_ge(gst[mi % 2].h, 16 * (mi // 2))
            wchain(c, "v", v)
            chain(c, "v", v, v.tensor_copy(
                gstrip[:, (mi % 2) * L + e:(mi % 2) * L + 2048:4],
                gtile[:, (it % 2) * 512:(it % 2) * 512 + 512]), sem=gsc)
    c.final.append((gst[0], gst[0].n))
    c.final.append((gst[1], gst[1].n))
    finish(c)


def p_l0wo(c, t):
    nc, blk = c.nc, c.block
    wld = c.sem()
    wo_sb = c.sb("wosb", [128, 16 * D], BF16)
    oc_sb = c.sb("wocb", [128, 2 * 16 * 512], BF16)
    ones_sb = c.sb("wones", [128, 1], BF16)
    gx_sb = c.sb("gxsb", [128, 4 * 512], F32)
    s1_sb = c.sb("s1sb", [128, 2 * 512], F32)
    x1_sb = c.sb("x1sb", [128, 2 * 512], F32)
    x1b_sb = c.sb("x1bsb", [128, 2 * 512], BF16)
    sqb_sb = c.sb("sqbsb", [128, 2 * 512], BF16)
    srow_sb = c.sb("srow", [1, 2 * 512], F32)
    banks = [c.ps("wops", [128, 512], F32) for _ in range(4)]
    sum_ps = c.ps("sump", [1, 512], F32)
    sq_ps = c.ps("sqp", [1, 512], F32)

    ocld = [c.sem(), c.sem()]
    gxld = [c.sem(), c.sem()]
    pr = c.sem()
    pf = [c.sem() for _ in range(4)]
    x1d = c.sem()
    x1bd = c.sem()
    sqd = c.sem()
    stm = c.sem()
    srowd = c.sem()
    x1st = [c.sem(), c.sem()]
    strow = c.sem()

    @blk.gpsimd
    def _(g):
        emit_preload(c, g, t["wo0"], NH * DH, D, wo_sb, wld)

    onld = c.sem()

    @blk.sync
    def _(q):
        dma(q, ones_sb[:], t["onescol"][:], onld)
        for n in range(4):
            if n >= 2:
                q.wait_ge(pr.h, 16 * (n - 1))
            dma(q, oc_sb[:, (n % 2) * 16 * 512:((n % 2) * 16 + 16) * 512]
                .rearrange("p (kc m) -> p kc m", m=512),
                t["aoT0"][:, n * 512:(n + 1) * 512]
                .rearrange("(kc p) m -> p kc m", p=128), ocld[n % 2])
            for mi in range(16):
                it = n * 16 + mi
                if it >= 2:
                    q.wait_ge(x1d.h, it - 1)
                par = it % 2
                dma(q, gx_sb[:, par * 1024:par * 1024 + 512],
                    t["g0"][mi * 128:(mi + 1) * 128, n * 512:(n + 1) * 512],
                    gxld[par])
                dma(q, gx_sb[:, par * 1024 + 512:par * 1024 + 1024],
                    t["xT"][mi * 128:(mi + 1) * 128, n * 512:(n + 1) * 512],
                    gxld[par])

    @blk.tensor
    def _(tt):
        tt.wait_ge(wld.h, 16)
        tt.wait_ge(onld.h, 16)
        for n in range(4):
            tt.wait_ge(ocld[n % 2].h, 16 * (n // 2 + 1))
            for mi in range(16):
                it = n * 16 + mi
                b = it % 4
                if it >= 4:
                    tt.wait_ge(pf[b].h, it // 4)
                for kc in range(16):
                    ins = tt.matmul(banks[b][:],
                                    wo_sb[:, kc * D + mi * 128:kc * D + mi * 128 + 128],
                                    oc_sb[:, ((n % 2) * 16 + kc) * 512:
                                          ((n % 2) * 16 + kc) * 512 + 512],
                                    start=(kc == 0), stop=(kc == 15))
                ins.then_inc(pr.h, 1)
                pr.n += 1
                tt.wait_ge(x1bd.h, it + 1)
                if mi == 0 and n >= 1:
                    tt.wait_ge(srowd.h, 2 * n)
                tt.matmul(sum_ps[:], ones_sb[:], x1b_sb[:, (it % 2) * 512:
                          (it % 2) * 512 + 512], start=(mi == 0), stop=(mi == 15))
                tt.wait_ge(sqd.h, it + 1)
                ins2 = tt.matmul(sq_ps[:], ones_sb[:], sqb_sb[:, (it % 2) * 512:
                                 (it % 2) * 512 + 512], start=(mi == 0),
                                 stop=(mi == 15))
                ins2.then_inc(stm.h, 1)
                stm.n += 1

    @blk.scalar
    def _(s):
        for n in range(4):
            for mi in range(16):
                it = n * 16 + mi
                b = it % 4
                s.wait_ge(pr.h, it + 1)
                if it >= 2:
                    s.wait_ge(x1d.h, it - 1)
                wchain(c, "s", s)
                chain(c, "s", s, s.copy(s1_sb[:, (it % 2) * 512:(it % 2) * 512 + 512],
                                        banks[b][:]), sem=pf[b])
                s.wait_ge(x1d.h, it + 1)
                if it >= 2:
                    s.wait_ge(stm.h, it - 1)
                wchain(c, "s", s)
                chain(c, "s", s, s.activation(sqb_sb[:, (it % 2) * 512:
                                              (it % 2) * 512 + 512],
                                              x1_sb[:, (it % 2) * 512:
                                                    (it % 2) * 512 + 512],
                                              AF.Square), sem=sqd)
                dma(s, t["x1T"][mi * 128:(mi + 1) * 128, n * 512:(n + 1) * 512],
                    x1_sb[:, (it % 2) * 512:(it % 2) * 512 + 512], x1st[it % 2])
            s.wait_ge(stm.h, (n + 1) * 16)
            if n >= 1:
                s.wait_ge(strow.h, 32 * n)
            wchain(c, "s", s)
            chain(c, "s", s, s.copy(srow_sb[0:1, 0:512], sum_ps[0:1, :]), sem=srowd)
            wchain(c, "s", s)
            chain(c, "s", s, s.copy(srow_sb[0:1, 512:1024], sq_ps[0:1, :]), sem=srowd)
            m = c.marker["s"]
            s.wait_ge(m[0].h, m[1])
            dma(s, t["st1"][0:1, n * 512:(n + 1) * 512], srow_sb[0:1, 0:512], strow)
            dma(s, t["st1"][1:2, n * 512:(n + 1) * 512], srow_sb[0:1, 512:1024], strow)

    @blk.vector
    def _(v):
        for n in range(4):
            for mi in range(16):
                it = n * 16 + mi
                par = it % 2
                v.wait_ge(pf[it % 4].h, it // 4 + 1)
                v.wait_ge(gxld[par].h, 32 * (it // 2 + 1))
                if it >= 2:
                    v.wait_ge(x1st[par].h, 16 * (it // 2))
                    v.wait_ge(stm.h, it - 1)
                wchain(c, "v", v)
                chain(c, "v", v, v.tensor_tensor(
                    x1_sb[:, par * 512:par * 512 + 512],
                    s1_sb[:, par * 512:par * 512 + 512],
                    gx_sb[:, par * 1024:par * 1024 + 512], op=ALU.mult))
                wchain(c, "v", v)
                chain(c, "v", v, v.tensor_tensor(
                    x1_sb[:, par * 512:par * 512 + 512],
                    x1_sb[:, par * 512:par * 512 + 512],
                    gx_sb[:, par * 1024 + 512:par * 1024 + 1024], op=ALU.add),
                    sem=x1d)
                wchain(c, "v", v)
                chain(c, "v", v, v.tensor_copy(
                    x1b_sb[:, par * 512:par * 512 + 512],
                    x1_sb[:, par * 512:par * 512 + 512]), sem=x1bd)
    c.final.append((x1st[0], x1st[0].n))
    c.final.append((x1st[1], x1st[1].n))
    c.final.append((strow, strow.n))
    finish(c)


def _emit_norm_bounds(c, t, srcs, negM_sb, scale, units_of, rope_split=True):
    nc, blk = c.nc, c.block
    sld = [c.sem(), c.sem()]
    sqd = c.sem()
    nmm = c.sem()
    nred = c.sem()
    bcd = c.sem()
    strip = c.sb("nbs", [128, 2 * L], BF16)
    sqs = c.sb("nbsq", [128, 2 * L], BF16)
    nrow = c.sb("nbrow", [1, 80], F32)
    tmp = c.sb("nbtmp", [1, 2], F32)
    ones_sb = c.sb("nbones", [128, 1], BF16)
    onesr = c.sb("nbonesr", [1, 128], F32)
    old = c.sem()
    nps = [c.ps("nbps", [1, 512], F32) for _ in range(2)]
    mps = c.ps("nbmps", [128, 16], F32)

    chunks = []
    for si, (dram, nch) in enumerate(srcs):
        for ch in range(nch):
            chunks.append((si, dram, ch))

    unit_lists = []
    mm_count = [0]
    for (si, dram, ch) in chunks:
        ul = [(32 * si + col, pb, pw) for (col, pb, pw) in units_of(ch)]
        unit_lists.append(ul)
        mm_count.append(mm_count[-1] + 4 * len(ul))

    @blk.sync
    def _(q):
        dma(q, ones_sb[:], t["onescol"][:], old)
        dma(q, onesr[:], t["onesrowf"][:], old)
        for i, (si, dram, ch) in enumerate(chunks):
            if i >= 2:
                q.wait_ge(nmm.h, mm_count[i - 1])
            dma(q, strip[:, (i % 2) * L:(i % 2) * L + L],
                dram[ch * 128:(ch + 1) * 128, :], sld[i % 2])

    @blk.scalar
    def _(s):
        for i in range(len(chunks)):
            p = (i % 2) * L
            s.wait_ge(sld[i % 2].h, 16 * (i // 2 + 1))
            if i >= 2:
                s.wait_ge(nmm.h, mm_count[i - 1])
            wchain(c, "s", s)
            chain(c, "s", s, s.activation(sqs[:, p:p + L], strip[:, p:p + L],
                                          AF.Square), sem=sqd)

    @blk.tensor
    def _(tt):
        tt.wait_ge(old.h, 32)
        g = 0
        for i in range(len(chunks)):
            p = (i % 2) * L
            tt.wait_ge(sqd.h, i + 1)
            for (col, pb, pw) in unit_lists[i]:
                for n in range(4):
                    b = g % 2
                    if g >= 2:
                        tt.wait_ge(nred.h, g - 1)
                    tt.matmul(nps[b][:], ones_sb[pb:pb + pw, 0:1],
                              sqs[pb:pb + pw, p + n * 512:p + n * 512 + 512],
                              start=True, stop=True).then_inc(nmm.h, 1)
                    nmm.n += 1
                    g += 1

    @blk.vector
    def _(v):
        wchain(c, "v", v)
        chain(c, "v", v, v.memset(nrow[0:1, 0:80], 0.0))
        g = 0
        for i in range(len(chunks)):
            for (col, pb, pw) in unit_lists[i]:
                for n in range(4):
                    b = g % 2
                    v.wait_ge(nmm.h, g + 1)
                    wchain(c, "v", v)
                    chain(c, "v", v, v.reduce_max(tmp[0:1, 0:1], nps[b][0:1, :],
                                                  axis=AX.X))
                    wchain(c, "v", v)
                    chain(c, "v", v, v.tensor_tensor(nrow[0:1, col:col + 1],
                                                     nrow[0:1, col:col + 1],
                                                     tmp[0:1, 0:1], op=ALU.max),
                          sem=nred)
                    g += 1
        if rope_split:
            wchain(c, "v", v)
            chain(c, "v", v, v.tensor_tensor(nrow[0:1, 64:80], nrow[0:1, 0:16],
                                             nrow[0:1, 16:32], op=ALU.add))
            wchain(c, "v", v)
            chain(c, "v", v, v.tensor_tensor(nrow[0:1, 32:48], nrow[0:1, 32:48],
                                             nrow[0:1, 48:64], op=ALU.add))
        else:
            wchain(c, "v", v)
            chain(c, "v", v, v.tensor_copy(nrow[0:1, 64:80], nrow[0:1, 0:16]))
        wchain(c, "v", v)
        chain(c, "v", v, v.tensor_tensor(nrow[0:1, 64:80], nrow[0:1, 64:80],
                                         nrow[0:1, 32:48], op=ALU.mult))
        wchain(c, "v", v)
        chain(c, "v", v, v.tensor_scalar(nrow[0:1, 64:80], nrow[0:1, 64:80],
                                         1e-20, None, op0=ALU.add), sem=nred)

    sqr2 = c.sem()
    @blk.scalar
    def _(s):
        s.wait_ge(nred.h, nred.n)
        wchain(c, "s", s)
        chain(c, "s", s, s.activation(nrow[0:1, 64:80], nrow[0:1, 64:80],
                                      AF.Sqrt), sem=sqr2)

    @blk.vector
    def _(v):
        v.wait_ge(sqr2.h, 1)
        wchain(c, "v", v)
        chain(c, "v", v, v.tensor_scalar(nrow[0:1, 64:80], nrow[0:1, 64:80],
                                         -scale, None, op0=ALU.mult), sem=nred)

    @blk.tensor
    def _(tt):
        tt.wait_ge(nred.h, nred.n)
        tt.matmul(mps[:], onesr[0:1, :], nrow[0:1, 64:80], start=True,
                  stop=True).then_inc(bcd.h, 1)
        bcd.n += 1

    @blk.scalar
    def _(s):
        s.wait_ge(bcd.h, 1)
        wchain(c, "s", s)
        chain(c, "s", s, s.copy(negM_sb[:], mps[:]), sem=sqd)
    c.bar.append((sqd, sqd.n))
    c.bar.append((nred, nred.n))
    c.bar.append((nmm, nmm.n))
    c.bar.append((bcd, bcd.n))


def p_l0attn(c, t):
    nc, blk = c.nc, c.block
    negM_sb = c.sb("negM", [128, 16], F32)

    def units_of(ch):
        if ch < 16:
            return [(ch, 0, 128)]
        h0 = 2 * (ch - 16)
        return [(16 + h0, 0, 64), (16 + h0 + 1, 64, 64)]

    with scope(c):
        _emit_norm_bounds(c, t, [(t["qT"], 24), (t["kT"], 24)], negM_sb,
                          SCALE0, units_of)
    barrier(c)

    kres = c.sem()
    qld = [c.sem(), c.sem()]
    sred = c.sem()
    expd = c.sem()
    pvd = c.sem()
    oready = c.sem()
    recd = c.sem()
    bcd = c.sem()
    bcsb = c.sem()
    aod = c.sem()
    aost = [c.sem(), c.sem()]
    old2 = c.sem()

    kb_sb = c.sb("akb", [128, 8 * L], BF16)
    kr_sb = c.sb("akr", [128, 4 * L], BF16)
    v_sb = c.sb("av", [128, 16 * 1024], BF16)
    qb_sb = c.sb("aqb", [128, 2 * 8 * 512], BF16)
    qr_sb = c.sb("aqr", [128, 2 * 4 * 512], BF16)
    pt_sb = c.sb("apt", [128, 2 * 512], BF16)
    rrow = c.sb("arr", [1, 512], F32)
    bc_sb = c.sb("abc", [128, 512], F32)
    ao_sb = c.sb("aao", [128, 2 * 512], BF16)
    ones_sb = c.sb("aones", [128, 1], BF16)
    onesr = c.sb("aonesr", [1, 128], F32)
    s_ps = [c.ps("asps", [128, 512], F32) for _ in range(2)]
    o_ps = [c.ps("aops", [128, 512], F32) for _ in range(2)]
    sum_ps = [c.ps("asum", [1, 512], F32) for _ in range(2)]
    b_ps = c.ps("abps", [128, 512], F32)

    @blk.sync
    def _(q):
        dma(q, ones_sb[:], t["onescol"][:], old2)
        dma(q, onesr[:], t["onesrowf"][:], old2)
        for hg in range(2):
            if hg == 1:
                q.wait_ge(aod.h, 32)
            dma(q, kb_sb[:].rearrange("p (j n) -> p j n", n=L),
                t["kT"][hg * 1024:(hg + 1) * 1024, :]
                .rearrange("(j p) n -> p j n", p=128), kres)
            dma(q, kr_sb[:].rearrange("p (j n) -> p j n", n=L),
                t["kT"][2048 + hg * 512:2048 + (hg + 1) * 512, :]
                .rearrange("(j p) n -> p j n", p=128), kres)
            dma(q, v_sb[:].rearrange("p (tc m) -> p tc m", m=1024),
                t["vtok"][:, hg * 1024:(hg + 1) * 1024]
                .rearrange("(tc p) m -> p tc m", p=128), kres)
            for qb in range(4):
                idx = hg * 4 + qb
                if idx >= 2:
                    q.wait_ge(sred.h, 128 * (idx - 1))
                par = idx % 2
                dma(q, qb_sb[:, par * 4096:par * 4096 + 4096]
                    .rearrange("p (j n) -> p j n", n=512),
                    t["qT"][hg * 1024:(hg + 1) * 1024, qb * 512:(qb + 1) * 512]
                    .rearrange("(j p) n -> p j n", p=128), qld[par])
                dma(q, qr_sb[:, par * 2048:par * 2048 + 2048]
                    .rearrange("p (j n) -> p j n", n=512),
                    t["qT"][2048 + hg * 512:2048 + (hg + 1) * 512,
                            qb * 512:(qb + 1) * 512]
                    .rearrange("(j p) n -> p j n", p=128), qld[par])

    @blk.tensor
    def _(tt):
        tt.wait_ge(old2.h, 32)
        for hg in range(2):
            tt.wait_ge(kres.h, 48 * (hg + 1))
            for qb in range(4):
                idx = hg * 4 + qb
                par = idx % 2
                tt.wait_ge(qld[par].h, 32 * (idx // 2 + 1))
                for j in range(8):
                    it = idx * 8 + j
                    po = it % 2
                    jc = j // 2
                    pb = 64 * (j % 2)
                    if it >= 2:
                        tt.wait_ge(aod.h, it - 1)
                        tt.wait_ge(recd.h, it - 1)
                    for kt in range(16):
                        g = it * 16 + kt
                        b2 = kt % 2
                        tt.matmul(s_ps[b2][:],
                                  kb_sb[:, j * L + kt * 128:j * L + kt * 128 + 128],
                                  qb_sb[:, par * 4096 + j * 512:
                                        par * 4096 + j * 512 + 512],
                                  start=True, stop=False)
                        tt.matmul(s_ps[b2][:],
                                  kr_sb[pb:pb + 64,
                                        jc * L + kt * 128:jc * L + kt * 128 + 128],
                                  qr_sb[pb:pb + 64, par * 2048 + jc * 512:
                                        par * 2048 + jc * 512 + 512],
                                  start=False, stop=True).then_inc(sred.h, 1)
                        sred.n += 1
                        if kt >= 1:
                            gp = g - 1
                            kp = kt - 1
                            tt.wait_ge(expd.h, gp + 1)
                            tt.matmul(o_ps[po][:],
                                      v_sb[:, kp * 1024 + j * 128:
                                           kp * 1024 + j * 128 + 128],
                                      pt_sb[:, (kp % 2) * 512:(kp % 2) * 512 + 512],
                                      start=(kp == 0), stop=(kp == 15))
                            tt.matmul(sum_ps[po][:], ones_sb[:],
                                      pt_sb[:, (kp % 2) * 512:(kp % 2) * 512 + 512],
                                      start=(kp == 0), stop=(kp == 15)) \
                                .then_inc(pvd.h, 1)
                            pvd.n += 1
                    g = it * 16 + 15
                    tt.wait_ge(expd.h, g + 1)
                    tt.matmul(o_ps[po][:],
                              v_sb[:, 15 * 1024 + j * 128:15 * 1024 + j * 128 + 128],
                              pt_sb[:, (15 % 2) * 512:(15 % 2) * 512 + 512],
                              start=False, stop=True).then_inc(oready.h, 1)
                    oready.n += 1
                    tt.matmul(sum_ps[po][:], ones_sb[:],
                              pt_sb[:, (15 % 2) * 512:(15 % 2) * 512 + 512],
                              start=False, stop=True).then_inc(pvd.h, 1)
                    pvd.n += 1
                    tt.wait_ge(recd.h, it + 1)
                    if it >= 1:
                        tt.wait_ge(bcsb.h, it)
                    tt.matmul(b_ps[:], onesr[0:1, :], rrow[0:1, :],
                              start=True, stop=True).then_inc(bcd.h, 1)
                    bcd.n += 1

    @blk.scalar
    def _(s):
        for hg in range(2):
            for qb in range(4):
                for j in range(8):
                    it = (hg * 4 + qb) * 8 + j
                    h = hg * 8 + j
                    po = it % 2
                    for kt in range(16):
                        g = it * 16 + kt
                        s.wait_ge(sred.h, g + 1)
                        if g >= 2:
                            s.wait_ge(pvd.h, g - 1)
                        wchain(c, "s", s)
                        chain(c, "s", s, s.activation(
                            pt_sb[:, (kt % 2) * 512:(kt % 2) * 512 + 512],
                            s_ps[kt % 2][:], AF.Exp, bias=negM_sb[:, h:h + 1],
                            scale=SCALE0), sem=expd)
                    s.wait_ge(bcd.h, it + 1)
                    if it >= 1:
                        s.wait_ge(aod.h, it)
                    wchain(c, "s", s)
                    chain(c, "s", s, s.copy(bc_sb[:], b_ps[:]), sem=bcsb)
                    s.wait_ge(aod.h, it + 1)
                    dma(s, t["aoT0"][h * 128:(h + 1) * 128,
                                     qb * 512:(qb + 1) * 512],
                        ao_sb[:, po * 512:po * 512 + 512], aost[po])

    @blk.vector
    def _(v):
        for hg in range(2):
            for qb in range(4):
                for j in range(8):
                    it = (hg * 4 + qb) * 8 + j
                    po = it % 2
                    v.wait_ge(pvd.h, (it + 1) * 16)
                    if it >= 1:
                        v.wait_ge(bcd.h, it)
                    wchain(c, "v", v)
                    chain(c, "v", v, v.reciprocal(rrow[0:1, :],
                                                  sum_ps[po][0:1, :]), sem=recd)
                    v.wait_ge(oready.h, it + 1)
                    v.wait_ge(bcsb.h, it + 1)
                    if it >= 2:
                        v.wait_ge(aost[po].h, 16 * (it // 2))
                    wchain(c, "v", v)
                    chain(c, "v", v, v.tensor_tensor(
                        ao_sb[:, po * 512:po * 512 + 512], o_ps[po][:],
                        bc_sb[:], op=ALU.mult), sem=aod)
    c.final.append((aost[0], aost[0].n))
    c.final.append((aost[1], aost[1].n))
    finish(c)


def p_l1qkv(c, t):
    nc, blk = c.nc, c.block
    xn1_sb = c.sb("xn1sb", [128, 16 * L], BF16)

    with scope(c):
        stld = c.sem()
        rowd = c.sem()
        sq2 = c.sem()
        bcd = c.sem()
        bcc = c.sem()
        xld = [c.sem(), c.sem()]
        afd = c.sem()
        xst = [c.sem(), c.sem()]
        sums = c.sb("l1sum", [1, L], F32)
        sqs = c.sb("l1sq", [1, L], F32)
        mrow = c.sb("l1m", [1, L], F32)
        rrow = c.sb("l1r", [1, L], F32)
        onesr = c.sb("l1or", [1, 128], F32)
        mbc = c.sb("l1mbc", [128, L], F32)
        rbc = c.sb("l1rbc", [128, L], F32)
        xstr = c.sb("l1xs", [128, 2 * L], F32)
        tmps = c.sb("l1tmp", [128, L], F32)
        bps = [c.ps("l1bps", [128, 512], F32) for _ in range(2)]

        @blk.sync
        def _(q):
            dma(q, sums[:], t["st1"][0:1, :], stld)
            dma(q, sqs[:], t["st1"][1:2, :], stld)
            dma(q, onesr[:], t["onesrowf"][:], stld)
            for dc in range(16):
                if dc >= 2:
                    q.wait_ge(afd.h, dc - 1)
                dma(q, xstr[:, (dc % 2) * L:(dc % 2) * L + L],
                    t["x1T"][dc * 128:(dc + 1) * 128, :], xld[dc % 2])

        @blk.vector
        def _(v):
            v.wait_ge(stld.h, 48)
            wchain(c, "v", v)
            chain(c, "v", v, v.tensor_scalar(mrow[:], sums[:], 1.0 / D, None,
                                             op0=ALU.mult))
            wchain(c, "v", v)
            chain(c, "v", v, v.tensor_scalar(rrow[:], sqs[:], 1.0 / D, None,
                                             op0=ALU.mult))
            wchain(c, "v", v)
            chain(c, "v", v, v.tensor_tensor(sums[:], mrow[:], mrow[:],
                                             op=ALU.mult))
            wchain(c, "v", v)
            chain(c, "v", v, v.tensor_tensor(rrow[:], rrow[:], sums[:],
                                             op=ALU.subtract))
            wchain(c, "v", v)
            chain(c, "v", v, v.tensor_scalar(rrow[:], rrow[:], EPS, None,
                                             op0=ALU.add), sem=rowd)

        @blk.scalar
        def _(s):
            s.wait_ge(rowd.h, 1)
            wchain(c, "s", s)
            chain(c, "s", s, s.activation(rrow[:], rrow[:], AF.Sqrt), sem=sq2)

        @blk.vector
        def _(v):
            v.wait_ge(sq2.h, 1)
            wchain(c, "v", v)
            chain(c, "v", v, v.reciprocal(rrow[:], rrow[:]), sem=rowd)

        @blk.tensor
        def _(tt):
            tt.wait_ge(rowd.h, 2)
            for n in range(4):
                if n >= 1:
                    tt.wait_ge(bcc.h, 2 * n)
                tt.matmul(bps[0][:], onesr[0:1, :], mrow[0:1, n * 512:(n + 1) * 512],
                          start=True, stop=True).then_inc(bcd.h, 1)
                bcd.n += 1
                tt.matmul(bps[1][:], onesr[0:1, :], rrow[0:1, n * 512:(n + 1) * 512],
                          start=True, stop=True).then_inc(bcd.h, 1)
                bcd.n += 1

        @blk.scalar
        def _(s):
            for n in range(4):
                s.wait_ge(bcd.h, 2 * n + 2)
                wchain(c, "s", s)
                chain(c, "s", s, s.copy(mbc[:, n * 512:(n + 1) * 512], bps[0][:]),
                      sem=bcc)
                wchain(c, "s", s)
                chain(c, "s", s, s.copy(rbc[:, n * 512:(n + 1) * 512], bps[1][:]),
                      sem=bcc)

        @blk.vector
        def _(v):
            v.wait_ge(bcc.h, 8)
            for dc in range(16):
                p = (dc % 2) * L
                v.wait_ge(xld[dc % 2].h, 16 * (dc // 2 + 1))
                if dc >= 2:
                    v.wait_ge(xst[dc % 2].h, 16 * (dc // 2))
                wchain(c, "v", v)
                chain(c, "v", v, v.tensor_tensor(tmps[:], xstr[:, p:p + L],
                                                 mbc[:], op=ALU.subtract))
                wchain(c, "v", v)
                chain(c, "v", v, v.tensor_tensor(xn1_sb[:, dc * L:(dc + 1) * L],
                                                 tmps[:], rbc[:], op=ALU.mult),
                      sem=afd)

        @blk.scalar
        def _(s):
            for dc in range(16):
                s.wait_ge(afd.h, dc + 1)
                dma(s, t["xn1T"][dc * 128:(dc + 1) * 128, :],
                    xn1_sb[:, dc * L:(dc + 1) * L], xst[dc % 2])
        c.final.append((xst[0], xst[0].n))
        c.final.append((xst[1], xst[1].n))
        c.bar.append((afd, afd.n))
        c.bar.append((xst[0], xst[0].n))
        c.bar.append((xst[1], xst[1].n))
    barrier(c)

    xn1 = lambda kc, n0, nw: xn1_sb[:, kc * L + n0:kc * L + n0 + nw]
    with scope(c):
        emit_gemm(c, name="qkv", wt=t["wqkvT"], K=D, M=3 * NH * DH, rhs=xn1,
                  out=t["qkvT"])
    barrier(c)
    rc = c.sb("r1c", [128, L], BF16)
    rs = c.sb("r1s", [128, L], BF16)
    rld = c.sem()
    @blk.sync
    def _(q):
        dma(q, rc[:], t["rope1c"][:], rld)
        dma(q, rs[:], t["rope1s"][:], rld)
    @blk.vector
    def _(v):
        v.wait_ge(rld.h, 32)
    with scope(c):
        _emit_rope_pass(c, t, t["qkvT"][0:NH * DH, :], 16, rc, rs, 64)
    barrier(c)
    with scope(c):
        _emit_rope_pass(c, t, t["qkvT"][NH * DH:2 * NH * DH, :], 16, rc, rs, 64)
    barrier(c)
    with scope(c):
        _emit_transpose_pass(c, t, t["qkvT"][2 * NH * DH:3 * NH * DH, :],
                             t["v1tok"], 16, 16)
    finish(c)


def p_l1attn(c, t):
    nc, blk = c.nc, c.block
    negM_sb = c.sb("negM1", [128, 16], F32)

    with scope(c):
        _emit_norm_bounds(c, t, [(t["qkvT"][0:2048, :], 16),
                                 (t["qkvT"][2048:4096, :], 16)], negM_sb,
                          SCALE1, lambda ch: [(ch, 0, 128)], rope_split=False)
    barrier(c)

    kres = c.sem()
    qld = [c.sem(), c.sem()]
    mld = c.sem()
    sred = c.sem()
    expd = c.sem()
    mskd = c.sem()
    pvd = c.sem()
    oready = c.sem()
    recd = c.sem()
    bcd = c.sem()
    bcsb = c.sem()
    aod = c.sem()
    aost = [c.sem(), c.sem()]
    old2 = c.sem()

    k1_sb = c.sb("bk1", [128, 16 * L], BF16)
    v1_sb = c.sb("bv1", [128, 16 * L], BF16)
    qq_sb = c.sb("bqq", [128, 2 * 16 * 128], BF16)
    pt_sb = c.sb("bpt", [128, 2 * 128], BF16)
    rrow = c.sb("brr", [1, 128], F32)
    bc_sb = c.sb("bbc", [128, 128], F32)
    ao_sb = c.sb("bao", [128, 2 * 16 * 128], BF16)
    mdg_sb = c.sb("bmd", [128, 128], BF16)
    mlf_sb = c.sb("bml", [128, 128], BF16)
    ones_sb = c.sb("bones", [128, 1], BF16)
    onesr = c.sb("bonesr", [1, 128], F32)
    s_ps = [c.ps("bsps", [128, 128], F32) for _ in range(2)]
    o_ps = [c.ps("bops", [128, 128], F32) for _ in range(2)]
    sum_ps = [c.ps("bsum", [1, 128], F32) for _ in range(2)]
    b_ps = c.ps("bbps", [128, 128], F32)

    kts_of = lambda qt: list(range(max(0, qt - 4), qt + 1))
    its = []
    for qt in range(16):
        for h in range(16):
            its.append((qt, h))
    cum_s = [0]
    for (qt, h) in its:
        cum_s.append(cum_s[-1] + len(kts_of(qt)))
    tile_masked = []
    for (qt, h) in its:
        for kt in kts_of(qt):
            tile_masked.append((kt == qt) or (kt == qt - 4))
    mask_pfx = [0]
    for m in tile_masked:
        mask_pfx.append(mask_pfx[-1] + (1 if m else 0))

    @blk.sync
    def _(q):
        dma(q, ones_sb[:], t["onescol"][:], old2)
        dma(q, onesr[:], t["onesrowf"][:], old2)
        dma(q, mdg_sb[:], t["maskdiag"][:], mld)
        dma(q, mlf_sb[:], t["maskleft"][:], mld)
        dma(q, k1_sb[:].rearrange("p (j n) -> p j n", n=L),
            t["qkvT"][2048:4096, :].rearrange("(j p) n -> p j n", p=128), kres)
        dma(q, v1_sb[:].rearrange("p (tc m) -> p tc m", m=L),
            t["v1tok"][:, :].rearrange("(tc p) m -> p tc m", p=128), kres)
        for qt in range(16):
            if qt >= 2:
                q.wait_ge(sred.h, cum_s[16 * (qt - 1)])
            par = qt % 2
            dma(q, qq_sb[:, par * 2048:par * 2048 + 2048]
                .rearrange("p (j n) -> p j n", n=128),
                t["qkvT"][0:2048, qt * 128:(qt + 1) * 128]
                .rearrange("(j p) n -> p j n", p=128), qld[par])

    @blk.tensor
    def _(tt):
        tt.wait_ge(old2.h, 32)
        tt.wait_ge(kres.h, 32)
        for ii, (qt, h) in enumerate(its):
            it = ii
            po = it % 2
            par = qt % 2
            kts = kts_of(qt)
            if h == 0:
                tt.wait_ge(qld[par].h, 16 * (qt // 2 + 1))
            if it >= 2:
                tt.wait_ge(aod.h, it - 1)
                tt.wait_ge(recd.h, it - 1)
            for ki, kt in enumerate(kts):
                g = cum_s[ii] + ki
                b2 = g % 2
                tt.matmul(s_ps[b2][:],
                          k1_sb[:, h * L + kt * 128:h * L + kt * 128 + 128],
                          qq_sb[:, par * 2048 + h * 128:par * 2048 + h * 128 + 128],
                          start=True, stop=True).then_inc(sred.h, 1)
                sred.n += 1
                if ki >= 1:
                    gp = g - 1
                    kp = kts[ki - 1]
                    if tile_masked[gp]:
                        tt.wait_ge(mskd.h, mask_pfx[gp + 1])
                    else:
                        tt.wait_ge(expd.h, gp + 1)
                    tt.matmul(o_ps[po][:],
                              v1_sb[:, kp * L + h * 128:kp * L + h * 128 + 128],
                              pt_sb[:, (gp % 2) * 128:(gp % 2) * 128 + 128],
                              start=(ki == 1), stop=False)
                    tt.matmul(sum_ps[po][:], ones_sb[:],
                              pt_sb[:, (gp % 2) * 128:(gp % 2) * 128 + 128],
                              start=(ki == 1), stop=False).then_inc(pvd.h, 1)
                    pvd.n += 1
            ki = len(kts) - 1
            kp = kts[ki]
            g = cum_s[ii] + ki
            if tile_masked[g]:
                tt.wait_ge(mskd.h, mask_pfx[g + 1])
            else:
                tt.wait_ge(expd.h, g + 1)
            tt.matmul(o_ps[po][:],
                      v1_sb[:, kp * L + h * 128:kp * L + h * 128 + 128],
                      pt_sb[:, (g % 2) * 128:(g % 2) * 128 + 128],
                      start=(len(kts) == 1), stop=True).then_inc(oready.h, 1)
            oready.n += 1
            tt.matmul(sum_ps[po][:], ones_sb[:],
                      pt_sb[:, (g % 2) * 128:(g % 2) * 128 + 128],
                      start=(len(kts) == 1), stop=True).then_inc(pvd.h, 1)
            pvd.n += 1
            tt.wait_ge(recd.h, it + 1)
            if it >= 1:
                tt.wait_ge(bcsb.h, it)
            tt.matmul(b_ps[:], onesr[0:1, :], rrow[0:1, :],
                      start=True, stop=True).then_inc(bcd.h, 1)
            bcd.n += 1

    @blk.scalar
    def _(s):
        s.wait_ge(mld.h, 32)
        nexp = 0
        for ii, (qt, h) in enumerate(its):
            it = ii
            po = it % 2
            kts = kts_of(qt)
            for ki, kt in enumerate(kts):
                g = cum_s[ii] + ki
                s.wait_ge(sred.h, g + 1)
                if g >= 2:
                    s.wait_ge(pvd.h, g - 1)
                wchain(c, "s", s)
                chain(c, "s", s, s.activation(
                    pt_sb[:, (g % 2) * 128:(g % 2) * 128 + 128],
                    s_ps[g % 2][:], AF.Exp, bias=negM_sb[:, h:h + 1],
                    scale=SCALE1), sem=expd)
            s.wait_ge(bcd.h, it + 1)
            if it >= 1:
                s.wait_ge(aod.h, it)
            wchain(c, "s", s)
            chain(c, "s", s, s.copy(bc_sb[:], b_ps[:]), sem=bcsb)
            if h == 15:
                s.wait_ge(aod.h, it + 1)
                dma(s, t["aoT1"][:, qt * 128:(qt + 1) * 128]
                    .rearrange("(j p) n -> p j n", p=128),
                    ao_sb[:, (qt % 2) * 2048:(qt % 2) * 2048 + 2048]
                    .rearrange("p (j n) -> p j n", n=128), aost[qt % 2])

    @blk.vector
    def _(v):
        for ii, (qt, h) in enumerate(its):
            it = ii
            po = it % 2
            kts = kts_of(qt)
            for ki, kt in enumerate(kts):
                g = cum_s[ii] + ki
                masked = (kt == qt) or (kt == qt - 4)
                if masked:
                    v.wait_ge(expd.h, g + 1)
                    wchain(c, "v", v)
                    msk = mdg_sb if kt == qt else mlf_sb
                    chain(c, "v", v, v.tensor_tensor(
                        pt_sb[:, (g % 2) * 128:(g % 2) * 128 + 128],
                        pt_sb[:, (g % 2) * 128:(g % 2) * 128 + 128],
                        msk[:], op=ALU.mult), sem=mskd)
            v.wait_ge(pvd.h, cum_s[ii + 1])
            if it >= 1:
                v.wait_ge(bcd.h, it)
            wchain(c, "v", v)
            chain(c, "v", v, v.reciprocal(rrow[0:1, :], sum_ps[po][0:1, :]),
                  sem=recd)
            v.wait_ge(oready.h, it + 1)
            v.wait_ge(bcsb.h, it + 1)
            if it >= 2 and h <= 1:
                v.wait_ge(aost[qt % 2].h, 16 * (qt // 2))
            wchain(c, "v", v)
            chain(c, "v", v, v.tensor_tensor(
                ao_sb[:, (qt % 2) * 2048 + h * 128:(qt % 2) * 2048 + h * 128 + 128],
                o_ps[po][:], bc_sb[:], op=ALU.mult), sem=aod)
    c.final.append((aost[0], aost[0].n))
    c.final.append((aost[1], aost[1].n))
    finish(c)


def _emit_lut_build(c, v, dst, idx_sb, lut_sb, width, final_sem=None):
    for k in range(16):
        wchain(c, "v", v)
        if k == 0:
            chain(c, "v", v, v.tensor_scalar(dst, idx_sb, 0.0, lut_sb[:, 0:1],
                                             op0=ALU.is_equal, op1=ALU.mult))
        else:
            chain(c, "v", v, v.tensor_scalar(c.plane[:, 0:width], idx_sb,
                                             float(k), lut_sb[:, k:k + 1],
                                             op0=ALU.is_equal, op1=ALU.mult))
            wchain(c, "v", v)
            chain(c, "v", v, v.tensor_tensor(dst, dst, c.plane[:, 0:width],
                                             op=ALU.add),
                  sem=(final_sem if k == 15 else None))


def p_l1ghost(c, t):
    nc, blk = c.nc, c.block
    lutld = c.sem()
    lbc = c.sem()
    lutrow = c.sb("lutr", [1, 16], F32)
    onesr = c.sb("gor", [1, 128], F32)
    lut_sb = c.sb("lutc", [128, 16], F32)
    c.plane = c.sb("plane", [128, 16 * 512], BF16)

    with scope(c):
        lps = c.ps("lutps", [128, 16], F32)

        @blk.sync
        def _(q):
            dma(q, lutrow[:], t["lut"][:], lutld)
            dma(q, onesr[:], t["onesrowf"][:], lutld)

        @blk.tensor
        def _(tt):
            tt.wait_ge(lutld.h, 32)
            tt.matmul(lps[:], onesr[0:1, :], lutrow[0:1, :], start=True,
                      stop=True).then_inc(lbc.h, 1)
            lbc.n += 1

        @blk.scalar
        def _(s):
            s.wait_ge(lbc.h, 1)
            wchain(c, "s", s)
            chain(c, "s", s, s.copy(lut_sb[:], lps[:]), sem=lbc)

    @blk.vector
    def _(v):
        v.wait_ge(lbc.h, 2)
    @blk.gpsimd
    def _(g):
        g.wait_ge(lbc.h, 2)

    blocks = [(i * 512, 512) for i in range(10)] + [(5120, 384)]
    sched = []
    cum_git = [0]
    mg_of = []
    git = 0
    mg = 0
    for bi, (o0, ow) in enumerate(blocks):
        nms = ow // 128
        for ms in range(nms):
            for half in (0, 1):
                for n in range(4):
                    sched.append((bi, o0, ow, mg, ms, half, n, git))
                    git += 1
            mg_of.append((bi, o0, ow, ms))
            mg += 1
        cum_git.append(git)
    NMG = mg

    with scope(c):
        xnld = c.sem()
        xn1_sb = c.sb("gxn", [128, 16 * L], BF16)
        gix_sb = c.sb("ggix", [128, 2 * 16 * 512], BF16)
        uix_sb = c.sb("guix", [128, 2 * 16 * 512], BF16)
        wg_sb = c.sb("gwg", [128, 16 * 512], BF16)
        wu_sb = c.sb("gwu", [128, 16 * 512], BF16)
        sg_sb = c.sb("gsg", [128, 4 * 512], BF16)
        uu_sb = c.sb("guu", [128, 2 * 512], BF16)
        h_sb = c.sb("gh", [128, 2 * L], BF16)
        idxld = [c.sem(), c.sem()]
        bld = c.sem()
        pr = c.sem()
        pf = [c.sem() for _ in range(8)]
        muld = c.sem()
        hst = [c.sem(), c.sem()]
        banks = [c.ps("gps", [128, 512], F32) for _ in range(8)]

        @blk.sync
        def _(q):
            emit_preload(c, q, t["xn1T"], D, L, xn1_sb, xnld, cast=False)

        @blk.gpsimd
        def _(g):
            for bi, (o0, ow) in enumerate(blocks):
                par = bi % 2
                if bi >= 2:
                    g.wait_ge(bld.h, 2 * (bi - 1))
                dma(g, gix_sb[:, par * 8192:par * 8192 + 16 * ow]
                    .rearrange("p (kc m) -> p kc m", m=ow),
                    t["gidxT"][0:D, o0:o0 + ow]
                    .rearrange("(kc p) m -> p kc m", p=128), idxld[par])
                dma(g, uix_sb[:, par * 8192:par * 8192 + 16 * ow]
                    .rearrange("p (kc m) -> p kc m", m=ow),
                    t["uidxT"][0:D, o0:o0 + ow]
                    .rearrange("(kc p) m -> p kc m", p=128), idxld[par])

        @blk.vector
        def _(v):
            mg0 = 0
            for bi, (o0, ow) in enumerate(blocks):
                par = bi % 2
                v.wait_ge(idxld[par].h, 32 * (bi // 2 + 1))
                if bi >= 1:
                    v.wait_ge(pr.h, cum_git[bi])
                _emit_lut_build(c, v, wg_sb[:, 0:16 * ow],
                                gix_sb[:, par * 8192:par * 8192 + 16 * ow],
                                lut_sb, 16 * ow, final_sem=bld)
                _emit_lut_build(c, v, wu_sb[:, 0:16 * ow],
                                uix_sb[:, par * 8192:par * 8192 + 16 * ow],
                                lut_sb, 16 * ow, final_sem=bld)
                nms = ow // 128
                for ms in range(nms):
                    mg = mg0 + ms
                    base = cum_git[bi] + ms * 8
                    parh = mg % 2
                    for n in range(4):
                        gu = base + 4 + n
                        v.wait_ge(pf[gu % 8].h, gu // 8 + 1)
                        if n == 0 and mg >= 2:
                            v.wait_ge(hst[parh].h, 16 * (mg // 2))
                        wchain(c, "v", v)
                        chain(c, "v", v, v.tensor_tensor(
                            h_sb[:, parh * L + n * 512:parh * L + n * 512 + 512],
                            sg_sb[:, n * 512:n * 512 + 512],
                            uu_sb[:, (n % 2) * 512:(n % 2) * 512 + 512],
                            op=ALU.mult), sem=muld)
                mg0 += nms

        @blk.tensor
        def _(tt):
            tt.wait_ge(xnld.h, 16)
            for (bi, o0, ow, mg, ms, half, n, git) in sched:
                if half == 0 and n == 0 and ms == 0:
                    tt.wait_ge(bld.h, 2 * (bi + 1))
                b = git % 8
                if git >= 8:
                    tt.wait_ge(pf[b].h, git // 8)
                wsb = wg_sb if half == 0 else wu_sb
                for kc in range(16):
                    ins = tt.matmul(
                        banks[b][:],
                        wsb[:, kc * ow + ms * 128:kc * ow + ms * 128 + 128],
                        xn1_sb[:, kc * L + n * 512:kc * L + n * 512 + 512],
                        start=(kc == 0), stop=(kc == 15))
                ins.then_inc(pr.h, 1)
                pr.n += 1

        @blk.scalar
        def _(s):
            for (bi, o0, ow, mg, ms, half, n, git) in sched:
                b = git % 8
                s.wait_ge(pr.h, git + 1)
                wchain(c, "s", s)
                if half == 0:
                    if mg >= 1:
                        s.wait_ge(muld.h, 4 * mg)
                    ins = s.activation(sg_sb[:, n * 512:n * 512 + 512],
                                       banks[b][:], AF.Silu)
                else:
                    if 4 * mg + n >= 2:
                        s.wait_ge(muld.h, 4 * mg + n - 1)
                    ins = s.copy(uu_sb[:, (n % 2) * 512:(n % 2) * 512 + 512],
                                 banks[b][:])
                chain(c, "s", s, ins, sem=pf[b])
                if half == 1 and n == 3:
                    s.wait_ge(muld.h, 4 * (mg + 1))
                    dma(s, t["hT"][o0 + ms * 128:o0 + ms * 128 + 128, :],
                        h_sb[:, (mg % 2) * L:(mg % 2) * L + L], hst[mg % 2])

        c.final.append((hst[0], hst[0].n))
        c.final.append((hst[1], hst[1].n))
        c.bar.append((hst[0], hst[0].n))
        c.bar.append((hst[1], hst[1].n))
        c.bar.append((muld, muld.n))
        c.bar.append((pr, pr.n))
    barrier(c)

    with scope(c):
        wd_sb = c.sb("dwd", [128, 43 * 1024], BF16)
        hq_sb = c.sb("dhq", [128, 2 * 43 * 512], BF16)
        dix_sb = c.sb("ddix", [128, 2 * 1024], BF16)
        g1t_sb = c.sb("dg1", [128, 2 * 512], F32)
        dixld = [c.sem(), c.sem()]
        wdd = c.sem()
        hqld = [c.sem(), c.sem()]
        pr2 = c.sem()
        evd2 = c.sem()
        g1st = [c.sem(), c.sem()]
        banks2 = [c.ps("dps2", [128, 512], F32) for _ in range(8)]

        for mh in range(2):
            @blk.gpsimd
            def _(g, mh=mh):
                for ic in range(43):
                    si = mh * 43 + ic
                    par = si % 2
                    if si >= 2:
                        g.wait_ge(wdd.h, si - 1)
                    dma(g, dix_sb[:, par * 1024:par * 1024 + 1024],
                        t["didxT"][ic * 128:(ic + 1) * 128,
                                   mh * 1024:(mh + 1) * 1024], dixld[par])

            @blk.vector
            def _(v, mh=mh):
                for ic in range(43):
                    si = mh * 43 + ic
                    par = si % 2
                    v.wait_ge(dixld[par].h, 16 * (si // 2 + 1))
                    if mh == 1 and ic == 0:
                        v.wait_ge(pr2.h, 32)
                    _emit_lut_build(c, v, wd_sb[:, ic * 1024:(ic + 1) * 1024],
                                    dix_sb[:, par * 1024:par * 1024 + 1024],
                                    lut_sb, 1024, final_sem=wdd)

            @blk.sync
            def _(q, mh=mh):
                for qb in range(4):
                    sq = mh * 4 + qb
                    par = sq % 2
                    if sq >= 2:
                        q.wait_ge(pr2.h, 8 * (sq - 1))
                    dma(q, hq_sb[:, par * 43 * 512:par * 43 * 512 + 43 * 512]
                        .rearrange("p (ic n) -> p ic n", n=512),
                        t["hT"][0:I1, qb * 512:(qb + 1) * 512]
                        .rearrange("(ic p) n -> p ic n", p=128), hqld[par])

            @blk.tensor
            def _(tt, mh=mh):
                tt.wait_ge(wdd.h, 43 * (mh + 1))
                for qb in range(4):
                    sq = mh * 4 + qb
                    par = sq % 2
                    tt.wait_ge(hqld[par].h, 16 * (sq // 2 + 1))
                    if sq >= 1:
                        tt.wait_ge(evd2.h, 8 * sq)
                    for ic in range(43):
                        for m in range(8):
                            ins = tt.matmul(
                                banks2[m][:],
                                wd_sb[:, ic * 1024 + m * 128:ic * 1024 + m * 128 + 128],
                                hq_sb[:, par * 43 * 512 + ic * 512:
                                      par * 43 * 512 + ic * 512 + 512],
                                start=(ic == 0), stop=(ic == 42))
                            if ic == 42:
                                ins.then_inc(pr2.h, 1)
                                pr2.n += 1

            @blk.scalar
            def _(s, mh=mh):
                for qb in range(4):
                    sq = mh * 4 + qb
                    for m in range(8):
                        u = sq * 8 + m
                        s.wait_ge(pr2.h, sq * 8 + m + 1)
                        if u >= 2:
                            s.wait_ge(g1st[u % 2].h, 16 * (u // 2))
                        wchain(c, "s", s)
                        chain(c, "s", s, s.activation(
                            g1t_sb[:, (u % 2) * 512:(u % 2) * 512 + 512],
                            banks2[m][:], AF.Silu), sem=evd2)
                        mk = c.marker["s"]
                        s.wait_ge(mk[0].h, mk[1])
                        dma(s, t["g1"][(mh * 8 + m) * 128:(mh * 8 + m) * 128 + 128,
                                       qb * 512:(qb + 1) * 512],
                            g1t_sb[:, (u % 2) * 512:(u % 2) * 512 + 512],
                            g1st[u % 2])
        c.final.append((g1st[0], g1st[0].n))
        c.final.append((g1st[1], g1st[1].n))
    finish(c)


def p_l1wo(c, t):
    nc, blk = c.nc, c.block
    wld = c.sem()
    idld = c.sem()
    wo_sb = c.sb("w1sb", [128, 16 * D], BF16)
    oc_sb = c.sb("w1cb", [128, 2 * 16 * 512], BF16)
    idn = c.sb("w1id", [128, 128], F32)
    gx_sb = c.sb("w1gx", [128, 4 * 512], F32)
    s1_sb = c.sb("w1s1", [128, 2 * 512], F32)
    y_sb = c.sb("w1y", [128, 2 * 512], F32)
    ystr = c.sb("w1ys", [128, 2 * 4 * 2048], F32)
    banks = [c.ps("w1ps", [128, 512], F32) for _ in range(4)]
    tps = [c.ps("w1tp", [128, 128], F32) for _ in range(2)]

    ocld = [c.sem(), c.sem()]
    gxld = [c.sem(), c.sem()]
    pr = c.sem()
    pf = [c.sem() for _ in range(4)]
    yd = c.sem()
    trd = c.sem()
    evy = c.sem()
    yst = [c.sem(), c.sem()]

    @blk.gpsimd
    def _(g):
        emit_preload(c, g, t["wo1T"], NH * DH, D, wo_sb, wld)

    @blk.sync
    def _(q):
        dma(q, idn[:], t["identf"][:], idld)
        for n in range(4):
            if n >= 2:
                q.wait_ge(pr.h, 16 * (n - 1))
            dma(q, oc_sb[:, (n % 2) * 16 * 512:((n % 2) * 16 + 16) * 512]
                .rearrange("p (kc m) -> p kc m", m=512),
                t["aoT1"][:, n * 512:(n + 1) * 512]
                .rearrange("(kc p) m -> p kc m", p=128), ocld[n % 2])
            for mi in range(16):
                it = n * 16 + mi
                if it >= 2:
                    q.wait_ge(yd.h, it - 1)
                par = it % 2
                dma(q, gx_sb[:, par * 1024:par * 1024 + 512],
                    t["g1"][mi * 128:(mi + 1) * 128, n * 512:(n + 1) * 512],
                    gxld[par])
                dma(q, gx_sb[:, par * 1024 + 512:par * 1024 + 1024],
                    t["x1T"][mi * 128:(mi + 1) * 128, n * 512:(n + 1) * 512],
                    gxld[par])

    @blk.tensor
    def _(tt):
        tt.wait_ge(wld.h, 16)
        tt.wait_ge(idld.h, 16)
        for n in range(4):
            tt.wait_ge(ocld[n % 2].h, 16 * (n // 2 + 1))
            for mi in range(16):
                it = n * 16 + mi
                b = it % 4
                if it >= 4:
                    tt.wait_ge(pf[b].h, it // 4)
                for kc in range(16):
                    ins = tt.matmul(banks[b][:],
                                    wo_sb[:, kc * D + mi * 128:kc * D + mi * 128 + 128],
                                    oc_sb[:, ((n % 2) * 16 + kc) * 512:
                                          ((n % 2) * 16 + kc) * 512 + 512],
                                    start=(kc == 0), stop=(kc == 15))
                ins.then_inc(pr.h, 1)
                pr.n += 1
                tt.wait_ge(yd.h, it + 1)
                for ss in range(4):
                    jj = it * 4 + ss
                    if jj >= 2:
                        tt.wait_ge(evy.h, jj - 1)
                    tt.transpose(tps[jj % 2][:],
                                 y_sb[:, (it % 2) * 512 + ss * 128:
                                      (it % 2) * 512 + ss * 128 + 128],
                                 idn[:]).then_inc(trd.h, 1)
                    trd.n += 1

    @blk.scalar
    def _(s):
        for n in range(4):
            for mi in range(16):
                it = n * 16 + mi
                b = it % 4
                s.wait_ge(pr.h, it + 1)
                if it >= 2:
                    s.wait_ge(yd.h, it - 1)
                wchain(c, "s", s)
                chain(c, "s", s, s.copy(s1_sb[:, (it % 2) * 512:(it % 2) * 512 + 512],
                                        banks[b][:]), sem=pf[b])
                if mi == 0 and n >= 2:
                    s.wait_ge(yst[n % 2].h, 64 * (n // 2))
                for ss in range(4):
                    jj = it * 4 + ss
                    s.wait_ge(trd.h, jj + 1)
                    wchain(c, "s", s)
                    chain(c, "s", s, s.copy(
                        ystr[:, ((n % 2) * 4 + ss) * 2048 + mi * 128:
                             ((n % 2) * 4 + ss) * 2048 + mi * 128 + 128],
                        tps[jj % 2][:]), sem=evy)
            mk = c.marker["s"]
            s.wait_ge(mk[0].h, mk[1])
            for ss in range(4):
                dma(s, t["y"][n * 512 + ss * 128:n * 512 + ss * 128 + 128, :],
                    ystr[:, ((n % 2) * 4 + ss) * 2048:((n % 2) * 4 + ss) * 2048 + 2048],
                    yst[n % 2])

    @blk.vector
    def _(v):
        for n in range(4):
            for mi in range(16):
                it = n * 16 + mi
                par = it % 2
                v.wait_ge(pf[it % 4].h, it // 4 + 1)
                v.wait_ge(gxld[par].h, 32 * (it // 2 + 1))
                if it >= 2:
                    v.wait_ge(trd.h, 4 * (it - 1))
                wchain(c, "v", v)
                chain(c, "v", v, v.tensor_tensor(
                    y_sb[:, par * 512:par * 512 + 512],
                    s1_sb[:, par * 512:par * 512 + 512],
                    gx_sb[:, par * 1024:par * 1024 + 512], op=ALU.mult))
                wchain(c, "v", v)
                chain(c, "v", v, v.tensor_tensor(
                    y_sb[:, par * 512:par * 512 + 512],
                    y_sb[:, par * 512:par * 512 + 512],
                    gx_sb[:, par * 1024 + 512:par * 1024 + 1024], op=ALU.add),
                    sem=yd)
    c.final.append((yst[0], yst[0].n))
    c.final.append((yst[1], yst[1].n))
    finish(c)


PHASES = [
    ("prep", ["x", "identf", "identb"], ["xT", "xnT"], p_prep),
    ("l0proj", ["xnT", "wdq", "wuq", "wqr", "wdkv", "wuk", "wkr", "wuv",
                "rope0c", "rope0s", "identb"], ["qT", "kT", "vT", "vtok"], p_l0proj),
    ("l0ffn", ["xnT", "gup", "dwn"], ["g0"], p_l0ffn),
    ("l0attn", ["qT", "kT", "vtok", "onescol", "onesrowf"], ["aoT0"], p_l0attn),
    ("l0wo", ["aoT0", "wo0", "g0", "xT", "onescol"], ["x1T", "st1"], p_l0wo),
    ("l1qkv", ["x1T", "st1", "wqkvT", "rope1c", "rope1s", "identb", "onesrowf"],
     ["xn1T", "qkvT", "v1tok"], p_l1qkv),
    ("l1attn", ["qkvT", "v1tok", "onescol", "onesrowf", "maskdiag", "maskleft"],
     ["aoT1"], p_l1attn),
    ("l1ghost", ["xn1T", "lut", "gidxT", "uidxT", "didxT", "onesrowf"],
     ["hT", "g1"], p_l1ghost),
    ("l1wo", ["aoT1", "wo1T", "g1", "x1T", "identf"], ["y"], p_l1wo),
]


def _build_phase(name, ins, outs, fn):
    nc = bass.Bass("TRN2", target_bir_lowering=False, debug=False)
    t = {}
    for n in ins:
        sh, dt = SHAPES[n]
        t[n] = nc.dram_tensor(n, sh, dt, kind="ExternalInput").ap()
    for n in outs:
        sh, dt = SHAPES[n]
        t[n] = nc.dram_tensor(n, sh, dt, kind="ExternalOutput").ap()
    with contextlib.ExitStack() as es:
        block = es.enter_context(nc.Block())
        c = Ctx(nc, es, block)
        fn(c, t)
    return nc


def _host_layout(inputs):
    cst = _constants()
    per = []
    for b in range(8):
        d = dict(cst)
        d["x"] = np.ascontiguousarray(inputs["L0_attn_x"][b])
        d["wdq"] = inputs["L0_attn_W_D_Q"]
        d["wuq"] = inputs["L0_attn_W_U_Q"]
        d["wqr"] = inputs["L0_attn_W_Q_R"]
        d["wdkv"] = inputs["L0_attn_W_D_KV"]
        d["wuk"] = inputs["L0_attn_W_U_K"]
        d["wkr"] = inputs["L0_attn_W_K_R"]
        d["wuv"] = inputs["L0_attn_W_U_V"]
        d["wo0"] = inputs["L0_attn_W_O"]
        d["gup"] = inputs["L0_ffn_gate_up_proj"]
        d["dwn"] = inputs["L0_ffn_down_proj"]
        d["wqkvT"] = np.ascontiguousarray(inputs["L1_attn_w_qkv"].T)
        d["wo1T"] = np.ascontiguousarray(inputs["L1_attn_w_o"].T)
        d["lut"] = np.ascontiguousarray(inputs["L1_ffn_lut"].reshape(1, 16))
        d["gidxT"] = np.ascontiguousarray(inputs["L1_ffn_gate_idx"].T)
        d["uidxT"] = np.ascontiguousarray(inputs["L1_ffn_up_idx"].T)
        d["didxT"] = np.ascontiguousarray(inputs["L1_ffn_down_idx"].T)
        per.append(d)
    return per


def kernel(**inputs):
    per = _host_layout({k: np.asarray(v) for k, v in inputs.items()})
    for (name, ins, outs, fn) in PHASES:
        nc = _build_phase(name, ins, outs, fn)
        in_maps = [{k: per[cix][k] for k in ins} for cix in range(8)]
        res = run_bass_kernel_spmd(nc, in_maps, core_ids=list(range(8)))
        for cix in range(8):
            for k in outs:
                per[cix][k] = np.asarray(res.results[cix][k])
    return np.stack([per[cix]["y"] for cix in range(8)], axis=0)
